# version 03
# baseline (speedup 1.0000x reference)
import numpy as np
from contextlib import ExitStack

import concourse.bass as bass
import concourse.mybir as mybir
from concourse.bass_types import AP
from concourse.bass_utils import run_bass_kernel_spmd

F32 = mybir.dt.float32
I32 = mybir.dt.int32
BF16 = mybir.dt.bfloat16
AF = mybir.ActivationFunctionType
ALU = mybir.AluOpType
AX = mybir.AxisListType

B, S, D = 4, 8192, 1024
H, DK, DV, LSEG = 16, 64, 64, 512
TC = 4096
TT = TC + 2
NSEG = TC // LSEG
NCH = D // 128
DFF0, NCH0 = 2816, 22
DFF1, NCH1 = 4096, 32
SC0 = 1.0 / 8.0
NEG = -1.0e30

_DEBUG_TAPS = ()
_PHASES = 99
_P2N = 128
_P2MODE = 'full'
_P3N = 512
_P3MODE = 'full'
_P3TAIL = True
_P1M = 8
_P4MODE = 'full'
_DUMPS = ()


def _ttiles(n, step, start=0):
    out = []
    t = start
    while t < n:
        out.append((t, min(step, n - t)))
        t += step
    return out


def build_nc(taps=()):
    nc = bass.Bass("TRN2", target_bir_lowering=False)
    T, V, A, G, Sy = nc.tensor, nc.vector, nc.scalar, nc.gpsimd, nc.sync

    xf = nc.dram_tensor("xf", [128, NCH, TT], F32, kind="ExternalInput").ap()
    xpre = nc.dram_tensor("xpre", [128, NCH, TC], F32, kind="ExternalInput").ap()
    flags = nc.dram_tensor("flags", [1, 4], F32, kind="ExternalInput").ap()
    wi = {}
    for nm, shp in [
        ("L0_attn_Wq", [D, D]), ("L0_attn_Wk", [D, D]), ("L0_attn_Wv", [D, D]),
        ("L0_attn_Wo", [D, D]), ("L0_attn_betas", [H, DV]),
        ("L0_attn_memory0", [H, DK, DV]), ("L0_attn_z0", [H, DK]),
        ("L0_ffn_w_gate", [DFF0, D]), ("L0_ffn_w_up", [DFF0, D]),
        ("L0_ffn_w_down", [D, DFF0]),
        ("L1_attn_Wq", [D, D]), ("L1_attn_bq", [1, D]),
        ("L1_attn_Wk", [D, D]), ("L1_attn_bk", [1, D]),
        ("L1_attn_Wv", [D, D]), ("L1_attn_bv", [1, D]),
        ("L1_attn_weights", [H, DK, DK]),
        ("L1_ffn_w1", [DFF1, D]), ("L1_ffn_conv_w", [DFF1, 3]),
        ("L1_ffn_conv_b", [1, DFF1]), ("L1_ffn_w2", [D, DFF1]),
    ]:
        wi[nm] = nc.dram_tensor(nm, shp, F32, kind="ExternalInput").ap()
    y = nc.dram_tensor("y", [128, NCH, TC], F32, kind="ExternalOutput").ap()
    dbg = {}
    for t_ in taps:
        dbg[t_] = nc.dram_tensor("dbg_" + t_, [128, NCH, TT], F32,
                                 kind="ExternalOutput").ap()
    dmp = {}
    for t_ in _DUMPS:
        dmp[t_] = nc.dram_tensor("dmp_" + t_, [128, 16256], F32,
                                 kind="ExternalOutput").ap()

    qdr = nc.dram_tensor("qdr", [128, NCH, TT], F32).ap()
    kdr = nc.dram_tensor("kdr", [128, NCH, TT], F32).ap()
    vdr = nc.dram_tensor("vdr", [128, NCH, TT], F32).ap()
    kpre = nc.dram_tensor("kpre", [128, NCH, TC], F32).ap()
    vpre = nc.dram_tensor("vpre", [128, NCH, TC], F32).ap()
    wgT = nc.dram_tensor("wgT", [128, NCH, DFF0], F32).ap()
    wuT = nc.dram_tensor("wuT", [128, NCH, DFF0], F32).ap()
    wdT = nc.dram_tensor("wdT", [128, NCH0, D], F32).ap()
    w1T = nc.dram_tensor("w1T", [128, NCH, DFF1], F32).ap()
    w2T = nc.dram_tensor("w2T", [128, NCH1, D], F32).ap()
    wqT1 = nc.dram_tensor("wqT1", [128, NCH, D], F32).ap()
    wkT1 = nc.dram_tensor("wkT1", [128, NCH, D], F32).ap()
    wvT1 = nc.dram_tensor("wvT1", [128, NCH, D], F32).ap()

    with ExitStack() as st:
        def sbt(name, shape, dt=F32):
            return st.enter_context(nc.sbuf_tensor(name, shape, dt))

        xs = sbt("xs", [128, NCH, TT])
        AR = 16256
        ar = sbt("arena", [128, AR])
        wb = sbt("wb", [128, NCH * 128])
        wnat = sbt("wnat", [128, 512])
        ident = sbt("ident", [128, 128])
        ones = sbt("ones", [128, 128])
        biasq = sbt("biasq", [128, NCH])
        biask = sbt("biask", [128, NCH])
        biasv = sbt("biasv", [128, NCH])
        convw = sbt("convw", [128, NCH1, 3])
        convb = sbt("convb", [128, NCH1])
        gates = sbt("gates", [128, H])
        phiw = sbt("phiw", [128, NCH * DK])
        stats = sbt("stats", [128, 64])
        MS = 68
        mst = sbt("mst", [128, H * MS])
        pb = [st.enter_context(nc.psum_tensor("pb%d" % i, [128, 512], F32))
              for i in range(8)]

        xsf = xs[:, :, :].rearrange("p a b -> p (a b)")
        iicol = wnat

        CH = st.enter_context(nc.semaphore("ch"))
        SD = st.enter_context(nc.semaphore("sd"))
        C = {"ch": 0, "sd": 0}

        def dma(out, in_):
            Sy.dma_start(out=out, in_=in_).then_inc(SD, 16)
            C["sd"] += 16

        def dmas(out, in_):
            Sy.dma_start(out=out, in_=in_,
                         allow_slow_non_contiguous=True).then_inc(SD, 16)
            C["sd"] += 16

        def step(instr):
            instr.then_inc(CH, 1)
            C["ch"] += 1

        def begin(eng):
            eng.wait_ge(CH, C["ch"])

        def sy_step():
            Sy.wait_ge(SD, C["sd"])
            step(Sy.nop())

        def arv(off, shape):
            n = int(np.prod(shape))
            assert off + n <= AR, (off, shape)
            a = ar[:, off:off + n]
            if len(shape) == 1:
                return a
            if len(shape) == 2:
                return a.rearrange("p (a b) -> p a b", b=shape[1])
            return a.rearrange("p (a b c) -> p a b c", b=shape[1], c=shape[2])

        qsg = arv(0, [256])
        ksg = arv(256, [256])
        kmsg = arv(512, [256])
        vsg = arv(768, [256])
        ktm = arv(1024, [256])
        VS = 68
        vtm = arv(1280, [4 * 68])
        sktm = arv(1552, [256])
        sq_ = arv(1808, [256])
        psb = arv(2064, [4, 512])
        ptsb = arv(4112, [512])
        attd = arv(4624, [512])
        nums = arv(5136, [512])
        stg = arv(5648, [256])
        denr = arv(5904, [512])
        ghatt = arv(6416, [NCH, 2])
        attfm = arv(6432, [NCH, 512])
        mask = arv(10528, [4, 512])
        pstg = arv(0, [512])
        xpre_sb = arv(512, [NCH, 512])
        m0tmp = arv(0, [H, 68])
        attfm_f = attfm.rearrange("p a b -> p (a b)")
        lnrows = arv(4112, [1536])
        sqb = psb.rearrange("p a b -> p (a b)")

        begin(Sy)
        dma(xs[:, :, :], xf)
        dmas(gates[0:64, 0:H], AP(wi["L0_attn_betas"].tensor, 0, [[1, DV], [DV, H]]))
        dmas(biasq[:, 0:NCH], AP(wi["L1_attn_bq"].tensor, 0, [[1, 128], [128, NCH]]))
        dmas(biask[:, 0:NCH], AP(wi["L1_attn_bk"].tensor, 0, [[1, 128], [128, NCH]]))
        dmas(biasv[:, 0:NCH], AP(wi["L1_attn_bv"].tensor, 0, [[1, 128], [128, NCH]]))
        dmas(convw[:, :, :], AP(wi["L1_ffn_conv_w"].tensor, 0,
                                [[3, 128], [3 * 128, NCH1], [1, 3]]))
        dmas(convb[:, 0:NCH1], AP(wi["L1_ffn_conv_b"].tensor, 0,
                                  [[1, 128], [128, NCH1]]))
        dma(stats[0:1, 0:4], flags)
        sy_step()
        begin(V)
        V.memset(stats[:, 21:22], 1e-5)
        step(V.drain())
        begin(A)
        step(A.activation(gates[0:64, 0:H], gates[0:64, 0:H], AF.Sigmoid))

        begin(G)
        G.iota(iicol[:, 0:128], pattern=[[1, 128]], base=0, channel_multiplier=-1,
               allow_small_or_imprecise_dtypes=True)
        step(G.drain())
        begin(V)
        V.tensor_single_scalar(ident[:, :], iicol[:, 0:128], 0.0, ALU.is_equal)
        V.memset(ones[:, :], 1.0)
        step(V.drain())
        begin(T)
        step(T.matmul(pb[0][:, 0:4], ones[0:1, 0:128], stats[0:1, 0:4],
                      start=True, stop=True))
        begin(V)
        V.tensor_copy(stats[:, 4:8], pb[0][:, 0:4])
        step(V.drain())

        begin(G)
        G.iota(iicol[:, 0:128].rearrange("p (a b) -> p a b", b=32),
               pattern=[[2, 4], [16, 32]], base=0, channel_multiplier=0,
               allow_small_or_imprecise_dtypes=True)
        step(G.drain())
        begin(T)
        step(T.transpose(pb[0][:, 0:128], iicol[:, 0:128], ident[:, :]))
        begin(V)
        V.tensor_copy(iicol[:, 0:1], pb[0][:, 0:1])
        step(V.drain())
        begin(G)
        G.iota(denr[:, :].rearrange("p (a b c) -> p a b c", b=8, c=32),
               pattern=[[1, 2], [2, 8], [16, 32]], base=0, channel_multiplier=0,
               allow_small_or_imprecise_dtypes=True)
        step(G.drain())
        begin(V)
        for blk in range(4):
            clo_i, ihalf = blk // 2, blk % 2
            off = float(8 * ihalf + clo_i)
            V.tensor_scalar(stats[:, 8 + blk:9 + blk], iicol[:, 0:1], off, None,
                            ALU.add)
        V.drain()
        for blk in range(4):
            V.scalar_tensor_tensor(mask[:, blk, :], denr[:, :],
                                   stats[:, 8 + blk:9 + blk], denr[:, :],
                                   ALU.subtract, ALU.bypass)
        V.drain()
        for blk in range(4):
            V.tensor_scalar(mask[:, blk, :], mask[:, blk, :], 0.0, NEG,
                            ALU.is_gt, ALU.mult)
        step(V.drain())

        main_tiles = _ttiles(TT, 512)
        pre_tiles = _ttiles(TC, 512)
        xpre_f = xpre_sb.rearrange("p a b -> p (a b)")

        def proj_pass(whbm, dst, src_is_stream, tiles):
            for m in range(_P1M):
                begin(Sy)
                for k in range(NCH):
                    dma(wb[:, k * 128:(k + 1) * 128],
                        whbm[k * 128:(k + 1) * 128, m * 128:(m + 1) * 128])
                sy_step()
                for (t0, tn) in tiles:
                    if not src_is_stream:
                        begin(Sy)
                        dma(xpre_sb[:, :, 0:tn], xpre[:, :, t0:t0 + tn])
                        sy_step()
                    begin(T)
                    for k in range(NCH):
                        rhs = (xsf[:, k * TT + t0: k * TT + t0 + tn]
                               if src_is_stream
                               else xpre_f[:, k * 512: k * 512 + tn])
                        T.matmul(pb[0][:, 0:tn], wb[:, k * 128:(k + 1) * 128],
                                 rhs, start=(k == 0), stop=(k == NCH - 1))
                    step(T.drain())
                    begin(V)
                    V.tensor_copy(pstg[:, 0:tn], pb[0][:, 0:tn])
                    step(V.drain())
                    begin(Sy)
                    dma(dst[:, m, t0:t0 + tn], pstg[:, 0:tn])
                    sy_step()

        if _PHASES >= 1:
            proj_pass(wi["L0_attn_Wq"], qdr, True, main_tiles)
            proj_pass(wi["L0_attn_Wk"], kdr, True, main_tiles)
            proj_pass(wi["L0_attn_Wv"], vdr, True, main_tiles)
            proj_pass(wi["L0_attn_Wk"], kpre, False, pre_tiles)
            proj_pass(wi["L0_attn_Wv"], vpre, False, pre_tiles)

        begin(V)
        V.memset(mst[:, :], 0.0)
        step(V.drain())

        def elu1(dst, src, tmp):
            begin(V)
            V.tensor_scalar_min(tmp, src, 0.0)
            step(V.drain())
            begin(A)
            A.activation(tmp, tmp, AF.Exp)
            step(A.drain())
            begin(V)
            V.scalar_tensor_tensor(dst, src, 0.0, tmp, ALU.max, ALU.add)
            step(V.drain())

        def load_head_kv(ksrc, vsrc, qsrc, col0, with_q):
            begin(Sy)
            dma(ksg[:, :].rearrange("p (a b) -> p a b", b=32),
                ksrc[:, :, col0:col0 + 32])
            dma(vsg[:, :].rearrange("p (a b) -> p a b", b=32),
                vsrc[:, :, col0:col0 + 32])
            if with_q:
                dma(qsg[:, :].rearrange("p (a b) -> p a b", b=32),
                    qsrc[:, :, col0:col0 + 32])
            sy_step()
            begin(Sy)
            dma(kmsg[0:64, :], ksg[64:128, :])
            dma(kmsg[64:128, :], ksg[0:64, :])
            sy_step()

        def head_tm(with_sk=True, parts='all'):
            for rnd, (src, dstt, stridew) in enumerate(
                    [(ksg, ktm, 64), (vsg, vtm, VS)]):
                begin(T)
                for L in range(4):
                    clo, g = L // 2, L % 2
                    idn = ident[64 * clo:64 * clo + 64, 64 * clo:64 * clo + 64]
                    T.transpose(pb[4 + L][:, 0:64],
                                src[64 * clo:64 * clo + 64,
                                    g * 128:(g + 1) * 128], idn)
                step(T.drain())
                begin(V)
                for L in range(4):
                    V.tensor_copy(dstt[:, L * stridew:L * stridew + 64],
                                  pb[4 + L][:, 0:64])
                step(V.drain())
            if parts == 'tp':
                return
            begin(V)
            V.memset(vtm.rearrange("p (a b) -> p a b", b=VS)[:, :, 64:65], 1.0)
            step(V.drain())
            if parts == 'ones':
                return
            if with_sk:
                elu1(sktm[:, :], ktm[:, :], denr[:, 0:256])

        def dm_update(h, mfac=None):
            begin(T)
            for base in range(2):
                for L in range(4):
                    T.matmul(pb[7][64 * base:64 * base + 64, 0:DV + 1],
                             sktm[:, L * 64:(L + 1) * 64],
                             vtm[:, L * VS:L * VS + 65],
                             start=(L == 0), stop=(L == 3))
            step(T.drain())
            begin(V)
            for base in range(2):
                sl = slice(64 * base, 64 * base + 64)
                if mfac is None:
                    V.tensor_add(mst[sl, h * MS:h * MS + 65],
                                 mst[sl, h * MS:h * MS + 65],
                                 pb[7][sl, 0:DV + 1])
                else:
                    V.scalar_tensor_tensor(mst[sl, h * MS:h * MS + 65],
                                           pb[7][sl, 0:DV + 1], mfac[sl, :],
                                           mst[sl, h * MS:h * MS + 65],
                                           ALU.mult, ALU.add)
            step(V.drain())

        fM = stats[:, 4:5]
        if _PHASES >= 2:
            cnt = 0
            for seg in range(NSEG):
                for h in range(H):
                    if cnt >= _P2N:
                        break
                    cnt += 1
                    load_head_kv(kpre, vpre, None, seg * LSEG + h * 32, False)
                    if _P2MODE == 'load':
                        continue
                    head_tm(with_sk=True,
                            parts=('all' if _P2MODE in ('tm', 'full') else _P2MODE))
                    if _P2MODE != 'full':
                        continue
                    dm_update(h, mfac=fM)

        begin(Sy)
        m0 = wi["L0_attn_memory0"]
        z0 = wi["L0_attn_z0"]
        dmas(m0tmp[0:64, :, 0:DK], AP(m0.tensor, 0, [[DV, 64], [DK * DV, H], [1, DV]]))
        dmas(m0tmp[64:128, :, 0:DK], AP(m0.tensor, 0, [[DV, 64], [DK * DV, H], [1, DV]]))
        dmas(m0tmp[0:64, :, DK:DK + 1], AP(z0.tensor, 0, [[1, 64], [DK, H]]))
        dmas(m0tmp[64:128, :, DK:DK + 1], AP(z0.tensor, 0, [[1, 64], [DK, H]]))
        sy_step()
        begin(V)
        V.tensor_add(mst[:, :], mst[:, :],
                     m0tmp[:, :, :].rearrange("p a b -> p (a b)"))
        step(V.drain())

        def softmax_block(blk):
            bank = pb[blk]
            begin(V)
            V.tensor_add(bank[:, :], bank[:, :], mask[:, blk, :])
            V.drain()
            V.reduce_max(stats[:, 16:17], bank[:, :], axis=AX.X, negate=True)
            V.drain()
            step(V.tensor_scalar_mul(stats[:, 17:18], stats[:, 16:17], SC0))
            begin(A)
            A.activation(psb[:, blk, :], bank[:, :], AF.Exp, scale=SC0,
                         bias=stats[:, 17:18], accum_out=stats[:, 18:19])
            A.drain()
            A.activation(stats[:, 19:20], stats[:, 18:19], AF.Ln)
            step(A.drain())
            begin(V)
            step(V.tensor_sub(stats[:, 20:21], stats[:, 17:18], stats[:, 19:20]))
            begin(A)
            A.activation(psb[:, blk, :], bank[:, :], AF.Exp, scale=SC0,
                         bias=stats[:, 20:21])
            step(A.drain())

        def attn_head(qtr, hq, h):
            begin(T)
            mode = _P3MODE
            for blk in range(4):
                clo_i, ihalf = blk // 2, blk % 2
                lhsT = qsg[64 * clo_i:64 * clo_i + 64,
                           ihalf * 128:(ihalf + 1) * 128]
                for clo_j in range(2):
                    srct = ksg if clo_j == clo_i else kmsg
                    rhs = srct[64 * clo_i:64 * clo_i + 64, 0:256]
                    T.matmul(pb[blk][:, clo_j * 256:clo_j * 256 + 256],
                             lhsT, rhs, start=(clo_j == 0), stop=(clo_j == 1))
            step(T.drain())
            if mode == 'scores':
                return
            head_tm(with_sk=True)
            if mode == 'tm':
                return
            for blk in range(4):
                softmax_block(blk)
            if mode == 'softmax':
                return
            for Jc in range(4):
                begin(T)
                for Ib in range(4):
                    T.transpose(pb[Ib][:, 0:128],
                                psb[:, Ib, Jc * 128:(Jc + 1) * 128], ident[:, :])
                step(T.drain())
                begin(V)
                for Ib in range(4):
                    V.tensor_copy(ptsb[:, Ib * 128:(Ib + 1) * 128],
                                  pb[Ib][:, 0:128])
                step(V.drain())
                begin(T)
                step(T.matmul(pb[4][0:DV, 0:512], vtm[:, Jc * VS:Jc * VS + 64],
                              ptsb[:, 0:512], start=(Jc == 0), stop=(Jc == 3)))
            begin(V)
            V.tensor_copy(attd[0:64, 0:512], pb[4][0:DV, 0:512])
            step(V.drain())
            if mode == 'pt':
                return
            for clo in range(2):
                sl = slice(64 * clo, 64 * clo + 64)
                elu1(sq_[sl, :], qsg[sl, :], denr[sl, 0:256])
            if mode == 'elu':
                return
            begin(T)
            T.matmul(pb[5][0:DK + 1, 0:256], mst[0:64, h * MS:h * MS + 65],
                     sq_[0:64, :], start=True, stop=True)
            T.drain()
            T.matmul(pb[5][0:DK + 1, 256:512], mst[64:128, h * MS:h * MS + 65],
                     sq_[64:128, :], start=True, stop=True)
            step(T.drain())
            begin(V)
            V.reciprocal(denr[64:65, 0:512], pb[5][64:65, 0:512])
            step(V.drain())
            begin(T)
            step(T.matmul(pb[6][0:64, 0:512], ones[64:65, 0:64],
                          denr[64:65, 0:512], start=True, stop=True))
            if mode == 'mem':
                return
            begin(V)
            V.tensor_copy(nums[0:64, 0:512], pb[5][0:64, 0:512])
            V.drain()
            V.tensor_mul(nums[0:64, 0:512], nums[0:64, 0:512], pb[6][0:64, 0:512])
            V.drain()
            V.tensor_sub(nums[0:64, 0:512], nums[0:64, 0:512], attd[0:64, 0:512])
            step(V.drain())
            begin(V)
            c0 = qtr * 128 + hq * 32
            V.scalar_tensor_tensor(
                attfm[0:64, :, c0:c0 + 32],
                nums[0:64, 0:256].rearrange("p (a b) -> p a b", b=32),
                gates[0:64, h:h + 1],
                attd[0:64, 0:256].rearrange("p (a b) -> p a b", b=32),
                ALU.mult, ALU.add)
            V.scalar_tensor_tensor(
                stg[0:64, 0:256], nums[0:64, 256:512], gates[0:64, h:h + 1],
                attd[0:64, 256:512], ALU.mult, ALU.add)
            step(V.drain())
            begin(Sy)
            dma(attfm[64:128, :, c0:c0 + 32],
                stg[0:64, 0:256].rearrange("p (a b) -> p a b", b=32))
            sy_step()
            dm_update(h)

        def emit_ln(t0, tn):
            for half in range(2):
                ca = 4 * half
                begin(V)
                for c in range(ca, ca + 4):
                    V.tensor_mul(sqb[:, (c - ca) * 512:(c - ca) * 512 + tn],
                                 xs[:, c, t0:t0 + tn], xs[:, c, t0:t0 + tn])
                step(V.drain())
                begin(T)
                for c in range(ca, ca + 4):
                    T.matmul(pb[6][0:1, 0:tn], ones[:, 0:1],
                             xsf[:, c * TT + t0: c * TT + t0 + tn],
                             start=(c == 0), stop=(c == 7))
                for c in range(ca, ca + 4):
                    T.matmul(pb[7][0:1, 0:tn], ones[:, 0:1],
                             sqb[:, (c - ca) * 512:(c - ca) * 512 + tn],
                             start=(c == 0), stop=(c == 7))
                step(T.drain())
            murow = lnrows[0:1, 0:512]
            varrow = lnrows[0:1, 512:1024]
            mrsrow = lnrows[0:1, 1024:1536]
            begin(V)
            V.tensor_scalar_mul(murow[:, 0:tn], pb[6][0:1, 0:tn], 1.0 / D)
            V.drain()
            V.tensor_mul(varrow[:, 0:tn], murow[:, 0:tn], murow[:, 0:tn])
            V.drain()
            V.scalar_tensor_tensor(varrow[:, 0:tn], pb[7][0:1, 0:tn], 1.0 / D,
                                   varrow[:, 0:tn], ALU.mult, ALU.subtract)
            step(V.drain())
            begin(A)
            A.activation(varrow[:, 0:tn], varrow[:, 0:tn], AF.Ln,
                         bias=stats[0:1, 21:22])
            A.drain()
            A.activation(varrow[:, 0:tn], varrow[:, 0:tn], AF.Exp, scale=-0.5)
            step(A.drain())
            begin(V)
            V.tensor_mul(mrsrow[:, 0:tn], murow[:, 0:tn], varrow[:, 0:tn])
            step(V.drain())
            begin(T)
            T.matmul(pb[4][:, 0:tn], ones[0:1, 0:128], varrow[:, 0:tn],
                     start=True, stop=True)
            T.matmul(pb[5][:, 0:tn], ones[0:1, 0:128], mrsrow[:, 0:tn],
                     start=True, stop=True)
            step(T.drain())
            begin(V)
            for c in range(NCH):
                V.tensor_mul(xs[:, c, t0:t0 + tn], xs[:, c, t0:t0 + tn],
                             pb[4][:, 0:tn])
            V.drain()
            for c in range(NCH):
                V.tensor_sub(xs[:, c, t0:t0 + tn], xs[:, c, t0:t0 + tn],
                             pb[5][:, 0:tn])
            step(V.drain())

        hcnt = 0
        for seg in (range(NSEG) if _PHASES >= 3 else []):
            for qtr in range(4):
                for hq in range(4):
                    if hcnt >= _P3N:
                        continue
                    hcnt += 1
                    h = qtr * 4 + hq
                    load_head_kv(kdr, vdr, qdr, 1 + seg * LSEG + h * 32, True)
                    attn_head(qtr, hq, h)
            if not _P3TAIL:
                continue
            segc = 1 + seg * LSEG
            for m in range(NCH):
                begin(Sy)
                for k in range(NCH):
                    dma(wb[:, k * 128:(k + 1) * 128],
                        wi["L0_attn_Wo"][k * 128:(k + 1) * 128,
                                         m * 128:(m + 1) * 128])
                sy_step()
                begin(T)
                for k in range(NCH):
                    T.matmul(pb[0][:, 0:512], wb[:, k * 128:(k + 1) * 128],
                             attfm_f[:, k * 512:(k + 1) * 512],
                             start=(k == 0), stop=(k == NCH - 1))
                step(T.drain())
                begin(V)
                step(V.tensor_add(xs[:, m, segc:segc + 512],
                                  xs[:, m, segc:segc + 512], pb[0][:, 0:512]))
            emit_ln(segc, 512)

        if "l0attn" in taps:
            begin(Sy)
            dma(dbg["l0attn"], xs[:, :, :])
            sy_step()

        wpA = arv(0, [512])
        wpB = arv(512, [512])
        wpC = arv(1024, [512])
        wpD = arv(1536, [512])
        wpS = arv(2048, [512])
        wpM = arv(2560, [128])

        MID = [0.25, 0.75, 1.25, 1.75, 2.5, 3.5, 5.0]

        def fp4_stair(asrc, chain, chain2, nb16, wide):
            begin(V)
            V.tensor_scalar(chain, asrc, MID[0], None, ALU.is_gt)
            V.drain()
            for j in (1, 2, 3):
                V.scalar_tensor_tensor(chain, asrc, MID[j], chain,
                                       ALU.is_gt, ALU.add)
                V.drain()
            V.tensor_scalar(chain2, asrc, MID[4], None, ALU.is_gt)
            V.drain()
            V.scalar_tensor_tensor(chain2, asrc, MID[5], chain2,
                                   ALU.is_gt, ALU.add)
            V.drain()
            V.tensor_scalar(asrc, asrc, MID[6], 4.0, ALU.is_gt, ALU.mult)
            V.drain()
            V.scalar_tensor_tensor(chain2, chain2, 2.0, asrc, ALU.mult, ALU.add)
            V.drain()
            V.tensor_add(chain, chain, chain2)
            step(V.drain())

        def fp4_quant_nat(w_ap, nb16):
            n = nb16 * 16
            bm = wpM[:, 0:nb16]
            sh = wpM[:, 32:32 + nb16]
            rr = wpM[:, 64:64 + nb16]
            begin(V)
            V.tensor_reduce(bm[:, :, None], w_ap.rearrange("p (a b) -> p a b", b=16),
                            axis=AX.X, op=ALU.max, apply_absolute_value=True)
            V.drain()
            V.tensor_scalar(sh, bm, 1.0 / 6.0, 1e-12, ALU.mult, ALU.max)
            V.drain()
            V.reciprocal(rr, sh)
            V.drain()
            V.tensor_scalar_mul(sh, sh, 0.5)
            step(V.drain())
            rb = AP(rr.tensor, rr.offset, [[AR, 128], [1, nb16], [0, 16]])
            sb_ = AP(sh.tensor, sh.offset, [[AR, 128], [1, nb16], [0, 16]])
            begin(V)
            V.tensor_tensor(wpA[:, 0:n].rearrange("p (a b) -> p a b", b=16),
                            w_ap.rearrange("p (a b) -> p a b", b=16), rb, ALU.mult)
            step(V.drain())
            begin(A)
            A.activation(wpB[:, 0:n], wpA[:, 0:n], AF.Sign)
            A.drain()
            A.activation(wpA[:, 0:n], wpA[:, 0:n], AF.Abs)
            step(A.drain())
            fp4_stair(wpA[:, 0:n], wpC[:, 0:n], wpD[:, 0:n], nb16, n)
            begin(V)
            V.tensor_tensor(wpC[:, 0:n].rearrange("p (a b) -> p a b", b=16),
                            wpC[:, 0:n].rearrange("p (a b) -> p a b", b=16),
                            sb_, ALU.mult)
            V.drain()
            V.tensor_mul(wpC[:, 0:n], wpC[:, 0:n], wpB[:, 0:n])
            step(V.drain())

        def prep_store_T(src_ap, dstT, n0, kglob0, nk):
            for g0 in range(0, nk, 4):
                gn = min(4, nk - g0)
                begin(T)
                for i in range(gn):
                    T.transpose(pb[i][:, 0:128],
                                src_ap[:, (g0 + i) * 128:(g0 + i + 1) * 128],
                                ident[:, :])
                step(T.drain())
                begin(V)
                for i in range(gn):
                    V.tensor_copy(wpS[:, i * 128:(i + 1) * 128], pb[i][:, 0:128])
                step(V.drain())
                begin(Sy)
                dma(dstT[:, kglob0 + g0:kglob0 + g0 + gn, n0:n0 + 128],
                    wpS[:, 0:gn * 128].rearrange("p (a b) -> p a b", b=128))
                sy_step()

        def prep_fp4_weight(whbm, dstT, NW, KW):
            for n0 in range(0, NW, 128):
                for k0 in range(0, KW, 512):
                    kn = min(512, KW - k0)
                    begin(Sy)
                    dma(wnat[:, 0:kn], whbm[n0:n0 + 128, k0:k0 + kn])
                    sy_step()
                    fp4_quant_nat(wnat[:, 0:kn], kn // 16)
                    prep_store_T(wpC, dstT, n0, k0 // 128, kn // 128)

        def prep_plain_weight(whbm, dstT, NW, KW):
            for n0 in range(0, NW, 128):
                for k0 in range(0, KW, 512):
                    kn = min(512, KW - k0)
                    begin(Sy)
                    dma(wnat[:, 0:kn], whbm[n0:n0 + 128, k0:k0 + kn])
                    sy_step()
                    begin(V)
                    V.tensor_copy(wpC[:, 0:kn], wnat[:, 0:kn])
                    step(V.drain())
                    prep_store_T(wpC, dstT, n0, k0 // 128, kn // 128)

        def prep_bit_weight(whbm, dstT, NW, KW, scol):
            acc = stats[:, 40:41]
            tmp = stats[:, 41:42]
            begin(V)
            V.memset(acc, 0.0)
            step(V.drain())
            for n0 in range(0, NW, 128):
                for k0 in range(0, KW, 512):
                    kn = min(512, KW - k0)
                    begin(Sy)
                    dma(wnat[:, 0:kn], whbm[n0:n0 + 128, k0:k0 + kn])
                    sy_step()
                    begin(V)
                    V.tensor_reduce(tmp, wnat[:, 0:kn], axis=AX.X, op=ALU.add,
                                    apply_absolute_value=True)
                    V.drain()
                    V.tensor_add(acc, acc, tmp)
                    step(V.drain())
            begin(T)
            step(T.matmul(pb[0][0:1, 0:1], ones[:, 0:1], acc, start=True, stop=True))
            begin(V)
            V.tensor_scalar(stats[0:1, 43:44], pb[0][0:1, 0:1], 1.0 / (NW * KW),
                            1e-5, ALU.mult, ALU.max)
            V.drain()
            V.reciprocal(stats[0:1, 44:45], stats[0:1, 43:44])
            V.drain()
            V.tensor_scalar_mul(stats[0:1, 45:46], stats[0:1, 43:44], -1.0)
            step(V.drain())
            begin(T)
            T.matmul(pb[0][:, 0:1], ones[0:1, 0:128], stats[0:1, 44:45],
                     start=True, stop=True)
            T.matmul(pb[1][:, 0:1], ones[0:1, 0:128], stats[0:1, 45:46],
                     start=True, stop=True)
            step(T.drain())
            begin(V)
            V.tensor_copy(stats[:, scol:scol + 1], pb[0][:, 0:1])
            V.tensor_copy(stats[:, scol + 1:scol + 2], pb[1][:, 0:1])
            step(V.drain())
            for n0 in range(0, NW, 128):
                for k0 in range(0, KW, 512):
                    kn = min(512, KW - k0)
                    begin(Sy)
                    dma(wnat[:, 0:kn], whbm[n0:n0 + 128, k0:k0 + kn])
                    sy_step()
                    begin(V)
                    V.tensor_scalar(wpA[:, 0:kn], wnat[:, 0:kn],
                                    stats[:, scol:scol + 1], None, ALU.mult)
                    V.drain()
                    V.tensor_scalar(wpB[:, 0:kn], wpA[:, 0:kn], 0.5, None,
                                    ALU.is_gt)
                    V.drain()
                    V.scalar_tensor_tensor(wpC[:, 0:kn], wpA[:, 0:kn], -0.5,
                                           wpB[:, 0:kn], ALU.is_lt, ALU.subtract)
                    V.drain()
                    V.tensor_scalar(wpC[:, 0:kn], wpC[:, 0:kn],
                                    stats[:, scol + 1:scol + 2], None, ALU.mult)
                    step(V.drain())
                    prep_store_T(wpC, dstT, n0, k0 // 128, kn // 128)

        if _PHASES >= 4:
            if _P4MODE in ('fp4g', 'fp4', 'full'):
                prep_fp4_weight(wi["L0_ffn_w_gate"], wgT, DFF0, D)
            if _P4MODE in ('fp4', 'full'):
                prep_fp4_weight(wi["L0_ffn_w_up"], wuT, DFF0, D)
                prep_fp4_weight(wi["L0_ffn_w_down"], wdT, D, DFF0)
            if _P4MODE in ('plain', 'full'):
                prep_plain_weight(wi["L1_attn_Wq"], wqT1, D, D)
                prep_plain_weight(wi["L1_attn_Wk"], wkT1, D, D)
                prep_plain_weight(wi["L1_attn_Wv"], wvT1, D, D)
            if _P4MODE in ('bit', 'full'):
                prep_bit_weight(wi["L1_ffn_w1"], w1T, DFF1, D, 46)
                prep_bit_weight(wi["L1_ffn_w2"], w2T, D, DFF1, 49)
            for h in (range(H) if _P4MODE in ('phi', 'full') else []):
                begin(Sy)
                dma(wnat[0:64, 0:64],
                    AP(wi["L1_attn_weights"].tensor, h * DK * DK,
                       [[DK, 64], [1, DK]]))
                sy_step()
                begin(T)
                step(T.transpose(pb[0][0:64, 0:64],
                                 wnat[0:64, 0:64], ident[0:64, 0:64]))
                if h % 2 == 0:
                    begin(V)
                    V.tensor_copy(phiw[0:64, (h // 2) * 64:(h // 2) * 64 + 64],
                                  pb[0][0:64, 0:64])
                    step(V.drain())
                else:
                    begin(V)
                    V.tensor_copy(wpS[0:64, 0:64], pb[0][0:64, 0:64])
                    step(V.drain())
                    begin(Sy)
                    dma(phiw[64:128, (h // 2) * 64:(h // 2) * 64 + 64],
                        wpS[0:64, 0:64])
                    sy_step()

        xqF = arv(0, [NCH, 384])
        XT = arv(3072, [512])
        FB1 = arv(3584, [512])
        FB2 = arv(4096, [512])
        FB3 = arv(4608, [512])
        FB4 = arv(5120, [512])
        hqb = arv(5648, [NCH0, 384])
        xqF_f = xqF.rearrange("p a b -> p (a b)")
        hqb_f = hqb.rearrange("p a b -> p (a b)")

        qm5 = arv(14720, [128])

        def quant_fp4_tm(buf, nsub, w, nb16_tot):
            n = nsub * 128
            bm = qm5[:, 0:nb16_tot]
            sh = qm5[:, 32:32 + nb16_tot]
            rr = qm5[:, 64:64 + nb16_tot]
            begin(V)
            V.tensor_reduce(bm[:, :, None],
                            buf[:, 0:n].rearrange("p (a b) -> p a b", b=16),
                            axis=AX.X, op=ALU.max, apply_absolute_value=True)
            V.drain()
            V.tensor_scalar(sh, bm, 1.0 / 6.0, 1e-12, ALU.mult, ALU.max)
            V.drain()
            V.reciprocal(rr, sh)
            V.drain()
            V.tensor_scalar_mul(sh, sh, 0.5)
            step(V.drain())
            rb = AP(rr.tensor, rr.offset, [[AR, 128], [1, nb16_tot], [0, 16]])
            sb_ = AP(sh.tensor, sh.offset, [[AR, 128], [1, nb16_tot], [0, 16]])
            begin(V)
            V.tensor_tensor(buf[:, 0:n].rearrange("p (a b) -> p a b", b=16),
                            buf[:, 0:n].rearrange("p (a b) -> p a b", b=16),
                            rb, ALU.mult)
            step(V.drain())
            begin(A)
            A.activation(FB2[:, 0:n], buf[:, 0:n], AF.Sign)
            A.drain()
            A.activation(buf[:, 0:n], buf[:, 0:n], AF.Abs)
            step(A.drain())
            fp4_stair(buf[:, 0:n], FB3[:, 0:n], FB4[:, 0:n], nb16_tot, n)
            begin(V)
            V.tensor_tensor(FB3[:, 0:n].rearrange("p (a b) -> p a b", b=16),
                            FB3[:, 0:n].rearrange("p (a b) -> p a b", b=16),
                            sb_, ALU.mult)
            V.drain()
            V.tensor_mul(buf[:, 0:n], FB3[:, 0:n], FB2[:, 0:n])
            step(V.drain())

        def evac_tp(dst128, bank, w):
            if w < 128:
                V.memset(dst128, 0.0)
                V.drain()
            V.tensor_copy(dst128[0:w, 0:128] if w < 128 else dst128,
                          pb[bank][0:w, 0:128])

        def xq_fp4(t0, tn, dst, dst_f):
            for s0 in range(0, tn, 128):
                w = min(128, tn - s0)
                for fh in range(2):
                    begin(T)
                    for i in range(4):
                        c = fh * 4 + i
                        T.transpose(pb[i][0:w, 0:128],
                                    xsf[:, c * TT + t0 + s0: c * TT + t0 + s0 + w],
                                    ident[:, :])
                    step(T.drain())
                    begin(V)
                    for i in range(4):
                        evac_tp(XT[:, i * 128:(i + 1) * 128], i, w)
                    step(V.drain())
                    quant_fp4_tm(XT, 4, w, 32)
                    begin(T)
                    for i in range(4):
                        c = fh * 4 + i
                        T.transpose(pb[i][:, 0:128],
                                    XT[:, i * 128:(i + 1) * 128], ident[:, :])
                    step(T.drain())
                    begin(V)
                    for i in range(4):
                        c = fh * 4 + i
                        V.tensor_copy(dst_f[:, c * 384 + s0: c * 384 + s0 + w],
                                      pb[i][0:128, 0:w])
                    step(V.drain())

        if _PHASES >= 5:
            for (t0, tn) in _ttiles(TT, 384):
                xq_fp4(t0, tn, xqF, xqF_f)
                if t0 == 0 and 'p5xq' in _DUMPS:
                    begin(Sy)
                    dma(dmp['p5xq'], ar[:, :])
                    sy_step()
                for nch in range(NCH0):
                    begin(Sy)
                    dma(wb[:, :].rearrange("p (a b) -> p a b", b=128),
                        wgT[:, :, nch * 128:(nch + 1) * 128])
                    sy_step()
                    begin(T)
                    for k in range(NCH):
                        T.matmul(pb[4][:, 0:tn], wb[:, k * 128:(k + 1) * 128],
                                 xqF_f[:, k * 384:k * 384 + tn],
                                 start=(k == 0), stop=(k == NCH - 1))
                    step(T.drain())
                    begin(Sy)
                    dma(wb[:, :].rearrange("p (a b) -> p a b", b=128),
                        wuT[:, :, nch * 128:(nch + 1) * 128])
                    sy_step()
                    begin(T)
                    for k in range(NCH):
                        T.matmul(pb[5][:, 0:tn], wb[:, k * 128:(k + 1) * 128],
                                 xqF_f[:, k * 384:k * 384 + tn],
                                 start=(k == 0), stop=(k == NCH - 1))
                    step(T.drain())
                    begin(A)
                    A.activation(XT[:, 0:tn], pb[4][:, 0:tn], AF.Silu)
                    step(A.drain())
                    begin(V)
                    V.tensor_mul(XT[:, 0:tn], XT[:, 0:tn], pb[5][:, 0:tn])
                    step(V.drain())
                    nsub = (tn + 127) // 128
                    begin(T)
                    for s in range(nsub):
                        w = min(128, tn - s * 128)
                        T.transpose(pb[[6, 7, 3][s % 3]][0:w, 0:128],
                                    XT[:, s * 128:s * 128 + w], ident[:, :])
                    step(T.drain())
                    begin(V)
                    for s in range(nsub):
                        w = min(128, tn - s * 128)
                        evac_tp(FB1[:, s * 128:(s + 1) * 128], [6, 7, 3][s % 3], w)
                    step(V.drain())
                    if t0 == 0 and nch == 0 and 'p5h0' in _DUMPS:
                        begin(Sy)
                        dma(dmp['p5h0'], ar[:, :])
                        sy_step()
                    quant_fp4_tm(FB1, nsub, 128, nsub * 8)
                    if t0 == 0 and nch == 0 and 'p5hq0' in _DUMPS:
                        begin(Sy)
                        dma(dmp['p5hq0'], ar[:, :])
                        sy_step()
                    begin(T)
                    for s in range(nsub):
                        T.transpose(pb[[6, 7, 3][s % 3]][:, 0:128],
                                    FB1[:, s * 128:(s + 1) * 128], ident[:, :])
                    step(T.drain())
                    begin(V)
                    for s in range(nsub):
                        w = min(128, tn - s * 128)
                        V.tensor_copy(hqb_f[:, nch * 384 + s * 128:
                                            nch * 384 + s * 128 + w],
                                      pb[[6, 7, 3][s % 3]][0:128, 0:w])
                    step(V.drain())
                if t0 == 0 and 'p5hqb' in _DUMPS:
                    begin(Sy)
                    dma(dmp['p5hqb'], ar[:, :])
                    sy_step()
                for m in range(NCH):
                    for kg in range(0, NCH0, 4):
                        gn = min(4, NCH0 - kg)
                        begin(Sy)
                        dma(wnat[:, 0:gn * 128].rearrange("p (a b) -> p a b", b=128),
                            wdT[:, kg:kg + gn, m * 128:(m + 1) * 128])
                        sy_step()
                        begin(T)
                        for i in range(gn):
                            T.matmul(pb[m][:, 0:tn],
                                     wnat[:, i * 128:(i + 1) * 128],
                                     hqb_f[:, (kg + i) * 384:(kg + i) * 384 + tn],
                                     start=(kg + i == 0),
                                     stop=(kg + i == NCH0 - 1))
                        step(T.drain())
                for m in range(NCH):
                    begin(V)
                    step(V.tensor_add(xs[:, m, t0:t0 + tn], xs[:, m, t0:t0 + tn],
                                      pb[m][:, 0:tn]))
                emit_ln(t0, tn)

        if "ffn0" in taps:
            begin(Sy)
            dma(dbg["ffn0"], xs[:, :, :])
            sy_step()

        qf1 = arv(5648, [NCH, 256])
        kf1 = arv(7696, [NCH, 256])
        vf1 = arv(9744, [NCH, 256])
        phq = arv(11792, [256])
        phk = arv(12048, [256])
        prod = arv(12304, [256])
        srow = arv(12560, [256])
        rsrow = arv(12816, [256])
        tv1 = arv(13072, [256])
        dum6 = arv(13328, [256])
        phiwQ = arv(13584, [512])
        phiwK = arv(14096, [512])
        qf1_f = qf1.rearrange("p a b -> p (a b)")
        kf1_f = kf1.rearrange("p a b -> p (a b)")
        vf1_f = vf1.rearrange("p a b -> p (a b)")

        def l1_proj(dstf, wT, bias_sb, t0, tn):
            for m in range(NCH):
                begin(Sy)
                dma(wb[:, :].rearrange("p (a b) -> p a b", b=128),
                    wT[:, :, m * 128:(m + 1) * 128])
                sy_step()
                begin(T)
                for k in range(NCH):
                    T.matmul(pb[0][:, 0:tn], wb[:, k * 128:(k + 1) * 128],
                             xsf[:, k * TT + t0: k * TT + t0 + tn],
                             start=(k == 0), stop=(k == NCH - 1))
                step(T.drain())
                begin(A)
                A.activation(dstf[:, m * 256:m * 256 + tn], pb[0][:, 0:tn],
                             AF.Identity, bias=bias_sb[:, m:m + 1])
                step(A.drain())

        if _PHASES >= 6:
            begin(V)
            V.memset(stats[:, 33:34], 0.0)
            V.memset(stats[:, 34:35], 0.0)
            step(V.drain())
            for (wTl, acc_col, b_sb) in [(wqT1, 33, biasq), (wkT1, 34, biask)]:
                for m in range(NCH):
                    begin(Sy)
                    dma(wb[:, :].rearrange("p (a b) -> p a b", b=128),
                        wTl[:, :, m * 128:(m + 1) * 128])
                    sy_step()
                    for (t0, tn) in _ttiles(TC, 512, start=1):
                        tn = min(tn, 4097 - t0)
                        begin(T)
                        for k in range(NCH):
                            T.matmul(pb[0][:, 0:tn], wb[:, k * 128:(k + 1) * 128],
                                     xsf[:, k * TT + t0: k * TT + t0 + tn],
                                     start=(k == 0), stop=(k == NCH - 1))
                        step(T.drain())
                        begin(A)
                        A.activation(XT[:, 0:tn], pb[0][:, 0:tn], AF.Square,
                                     bias=b_sb[:, m:m + 1],
                                     accum_out=stats[:, 35:36])
                        step(A.drain())
                        begin(V)
                        V.tensor_add(stats[:, acc_col:acc_col + 1],
                                     stats[:, acc_col:acc_col + 1],
                                     stats[:, 35:36])
                        step(V.drain())
            LN8 = 2.0794415416798357
            for (acc_col, rq_col, ls_col) in [(33, 36, 37), (34, 38, 39)]:
                begin(T)
                step(T.matmul(pb[0][0:1, 0:1], ones[:, 0:1],
                              stats[:, acc_col:acc_col + 1], start=True, stop=True))
                begin(V)
                V.tensor_copy(stats[0:1, 60:61], pb[0][0:1, 0:1])
                step(V.drain())
                begin(A)
                A.activation(stats[0:1, 61:62], stats[0:1, 60:61], AF.Ln)
                A.drain()
                A.activation(stats[0:1, 61:62], stats[0:1, 61:62], AF.Exp,
                             scale=-0.5)
                step(A.drain())
                begin(V)
                V.tensor_mul(stats[0:1, 62:63], stats[0:1, 61:62],
                             stats[0:1, 61:62])
                V.drain()
                V.tensor_mul(stats[0:1, 62:63], stats[0:1, 62:63],
                             stats[0:1, 60:61])
                V.drain()
                V.tensor_scalar(stats[0:1, 62:63], stats[0:1, 62:63], -0.5,
                                -LN8, ALU.mult, ALU.add)
                step(V.drain())
                begin(T)
                T.matmul(pb[0][:, 0:1], ones[0:1, 0:128], stats[0:1, 61:62],
                         start=True, stop=True)
                T.matmul(pb[1][:, 0:1], ones[0:1, 0:128], stats[0:1, 62:63],
                         start=True, stop=True)
                step(T.drain())
                begin(V)
                V.tensor_copy(stats[:, rq_col:rq_col + 1], pb[0][:, 0:1])
                V.tensor_copy(stats[:, ls_col:ls_col + 1], pb[1][:, 0:1])
                step(V.drain())
            begin(V)
            V.tensor_scalar(phiwQ[:, :], phiw[:, :], stats[:, 36:37], None,
                            ALU.mult)
            V.tensor_scalar(phiwK[:, :], phiw[:, :], stats[:, 38:39], None,
                            ALU.mult)
            step(V.drain())
            for (t0, tn) in [(1 + i * 256, 256) for i in range(16)]:
                l1_proj(qf1_f, wqT1, biasq, t0, tn)
                l1_proj(kf1_f, wkT1, biask, t0, tn)
                l1_proj(vf1_f, wvT1, biasv, t0, tn)
                for h in range(H):
                    hb = 64 * (h % 2)
                    hsl = slice(hb, hb + 64)
                    hcol = (h // 2) * 256
                    begin(T)
                    T.matmul(pb[0][0:64, 0:tn],
                             phiwQ[hsl, (h // 2) * 64:(h // 2) * 64 + 64],
                             qf1_f[hsl, hcol:hcol + tn], start=True, stop=True)
                    T.drain()
                    T.matmul(pb[1][0:64, 0:tn],
                             phiwK[hsl, (h // 2) * 64:(h // 2) * 64 + 64],
                             kf1_f[hsl, hcol:hcol + tn], start=True, stop=True)
                    step(T.drain())
                    begin(A)
                    A.activation(phq[0:64, 0:tn], pb[0][0:64, 0:tn], AF.Exp,
                                 bias=stats[0:64, 37:38])
                    A.drain()
                    A.activation(phk[0:64, 0:tn], pb[1][0:64, 0:tn], AF.Exp,
                                 bias=stats[0:64, 39:40])
                    step(A.drain())
                    begin(V)
                    V.tensor_mul(prod[0:64, 0:tn], phq[0:64, 0:tn],
                                 phk[0:64, 0:tn])
                    step(V.drain())
                    begin(T)
                    step(T.matmul(pb[2][0:1, 0:tn], ones[0:64, 0:1],
                                  prod[0:64, 0:tn], start=True, stop=True))
                    begin(V)
                    V.tensor_copy(srow[0:1, 0:tn], pb[2][0:1, 0:tn])
                    V.drain()
                    V.reciprocal(rsrow[0:1, 0:tn], srow[0:1, 0:tn])
                    step(V.drain())
                    begin(T)
                    T.matmul(pb[3][hb:hb + 64, 0:tn], ones[0:1, 0:64],
                             srow[0:1, 0:tn], start=True, stop=True)
                    T.drain()
                    T.matmul(pb[4][hb:hb + 64, 0:tn], ones[0:1, 0:64],
                             rsrow[0:1, 0:tn], start=True, stop=True)
                    step(T.drain())
                    begin(V)
                    V.tensor_mul(tv1[hsl, 0:tn],
                                 vf1_f[hsl, hcol:hcol + tn],
                                 pb[3][hsl, 0:tn])
                    V.drain()
                    V.tensor_mul(tv1[hsl, 0:tn], tv1[hsl, 0:tn],
                                 pb[4][hsl, 0:tn])
                    V.drain()
                    V.tensor_add(xs[hsl, h // 2, t0:t0 + tn],
                                 xs[hsl, h // 2, t0:t0 + tn], tv1[hsl, 0:tn])
                    step(V.drain())
                emit_ln(t0, tn)

        if "attn1" in taps:
            begin(Sy)
            dma(dbg["attn1"], xs[:, :, :])
            sy_step()

        ari = sbt("ari", [128, 512], I32)
        hb7 = arv(5648, [NCH1, 322])
        xq7 = arv(0, [NCH, 322])
        hb7_f = hb7.rearrange("p a b -> p (a b)")
        xq7_f = xq7.rearrange("p a b -> p (a b)")

        def act_quant_tm(buf, n, maxcol, rscol):
            begin(V)
            V.tensor_scalar(FB2[:, 0:n], buf[:, 0:n],
                            stats[:, maxcol + 1:maxcol + 2], 0.5,
                            ALU.mult, ALU.add)
            V.drain()
            V.tensor_copy(ari[:, 0:n], FB2[:, 0:n])
            V.drain()
            V.tensor_copy(FB3[:, 0:n], ari[:, 0:n])
            V.drain()
            V.tensor_tensor(FB4[:, 0:n], FB3[:, 0:n], FB2[:, 0:n], ALU.is_gt)
            V.drain()
            V.tensor_sub(FB3[:, 0:n], FB3[:, 0:n], FB4[:, 0:n])
            V.drain()
            V.tensor_scalar(FB3[:, 0:n], FB3[:, 0:n], -128.0, 127.0,
                            ALU.max, ALU.min)
            V.drain()
            V.tensor_scalar(buf[:, 0:n], FB3[:, 0:n],
                            stats[:, rscol:rscol + 1], None, ALU.mult)
            step(V.drain())

        def xq_l1(t0c, tnc, dst_f, W_chunks, maxcol, rscol):
            for s0 in range(0, tnc, 128):
                w = min(128, tnc - s0)
                begin(V)
                V.memset(stats[:, maxcol:maxcol + 1], 0.0)
                step(V.drain())
                for fh in range(W_chunks // 4):
                    begin(T)
                    for i in range(4):
                        c = fh * 4 + i
                        T.transpose(pb[i][0:w, 0:128],
                                    xsf[:, c * TT + t0c + s0: c * TT + t0c + s0 + w],
                                    ident[:, :])
                    step(T.drain())
                    begin(V)
                    for i in range(4):
                        evac_tp(XT[:, i * 128:(i + 1) * 128], i, w)
                    step(V.drain())
                    begin(V)
                    V.tensor_reduce(stats[:, maxcol + 2:maxcol + 3],
                                    XT[:, 0:512], axis=AX.X, op=ALU.max,
                                    apply_absolute_value=True)
                    V.drain()
                    V.tensor_max(stats[:, maxcol:maxcol + 1],
                                 stats[:, maxcol:maxcol + 1],
                                 stats[:, maxcol + 2:maxcol + 3])
                    step(V.drain())
                begin(V)
                V.tensor_scalar(stats[:, maxcol:maxcol + 1],
                                stats[:, maxcol:maxcol + 1], 1e-5, None, ALU.max)
                V.drain()
                V.reciprocal(stats[:, maxcol + 1:maxcol + 2],
                             stats[:, maxcol:maxcol + 1])
                V.drain()
                V.tensor_scalar_mul(stats[:, maxcol + 1:maxcol + 2],
                                    stats[:, maxcol + 1:maxcol + 2], 127.0)
                V.drain()
                V.tensor_scalar_mul(stats[:, rscol:rscol + 1],
                                    stats[:, maxcol:maxcol + 1], 1.0 / 127.0)
                step(V.drain())
                for fh in range(W_chunks // 4):
                    begin(T)
                    for i in range(4):
                        c = fh * 4 + i
                        T.transpose(pb[i][0:w, 0:128],
                                    xsf[:, c * TT + t0c + s0: c * TT + t0c + s0 + w],
                                    ident[:, :])
                    step(T.drain())
                    begin(V)
                    for i in range(4):
                        evac_tp(XT[:, i * 128:(i + 1) * 128], i, w)
                    step(V.drain())
                    act_quant_tm(XT, 512, maxcol, rscol)
                    begin(T)
                    for i in range(4):
                        T.transpose(pb[i][:, 0:128],
                                    XT[:, i * 128:(i + 1) * 128], ident[:, :])
                    step(T.drain())
                    begin(V)
                    for i in range(4):
                        c = fh * 4 + i
                        V.tensor_copy(dst_f[:, c * 322 + s0: c * 322 + s0 + w],
                                      pb[i][0:128, 0:w])
                    step(V.drain())

        if _PHASES >= 7:
            SN7 = 320
            tiles7 = []
            t = 1
            while t < 1 + TC:
                n = min(SN7, 1 + TC - t)
                tiles7.append((t, n))
                t += n
            for (t0, tn) in tiles7:
                hn = tn + 2
                xq_l1(t0 - 1, hn, xq7_f, NCH, 50, 52)
                for nch in range(NCH1):
                    begin(Sy)
                    dma(wb[:, :].rearrange("p (a b) -> p a b", b=128),
                        w1T[:, :, nch * 128:(nch + 1) * 128])
                    sy_step()
                    begin(T)
                    for k in range(NCH):
                        T.matmul(pb[4][:, 0:hn], wb[:, k * 128:(k + 1) * 128],
                                 xq7_f[:, k * 322:k * 322 + hn],
                                 start=(k == 0), stop=(k == NCH - 1))
                    step(T.drain())
                    begin(V)
                    V.tensor_scalar(FB2[:, 0:tn], pb[4][:, 0:tn],
                                    convw[:, nch, 0:1], None, ALU.mult)
                    V.drain()
                    V.scalar_tensor_tensor(FB2[:, 0:tn], pb[4][:, 1:1 + tn],
                                           convw[:, nch, 1:2], FB2[:, 0:tn],
                                           ALU.mult, ALU.add)
                    V.drain()
                    V.scalar_tensor_tensor(FB2[:, 0:tn], pb[4][:, 2:2 + tn],
                                           convw[:, nch, 2:3], FB2[:, 0:tn],
                                           ALU.mult, ALU.add)
                    step(V.drain())
                    begin(A)
                    A.activation(hb7_f[:, nch * 322:nch * 322 + tn],
                                 FB2[:, 0:tn], AF.Silu,
                                 bias=convb[:, nch:nch + 1])
                    step(A.drain())
                for s0 in range(0, tn, 128):
                    w = min(128, tn - s0)
                    begin(V)
                    V.memset(stats[:, 54:55], 0.0)
                    step(V.drain())
                    for fh in range(NCH1 // 4):
                        begin(T)
                        for i in range(4):
                            c = fh * 4 + i
                            T.transpose(pb[i][0:w, 0:128],
                                        hb7_f[:, c * 322 + s0: c * 322 + s0 + w],
                                        ident[:, :])
                        step(T.drain())
                        begin(V)
                        for i in range(4):
                            evac_tp(XT[:, i * 128:(i + 1) * 128], i, w)
                        step(V.drain())
                        begin(V)
                        V.tensor_reduce(stats[:, 56:57], XT[:, 0:512],
                                        axis=AX.X, op=ALU.max,
                                        apply_absolute_value=True)
                        V.drain()
                        V.tensor_max(stats[:, 54:55], stats[:, 54:55],
                                     stats[:, 56:57])
                        step(V.drain())
                    begin(V)
                    V.tensor_scalar(stats[:, 54:55], stats[:, 54:55], 1e-5,
                                    None, ALU.max)
                    V.drain()
                    V.reciprocal(stats[:, 55:56], stats[:, 54:55])
                    V.drain()
                    V.tensor_scalar_mul(stats[:, 55:56], stats[:, 55:56], 127.0)
                    V.drain()
                    V.tensor_scalar_mul(stats[:, 57:58], stats[:, 54:55],
                                        1.0 / 127.0)
                    step(V.drain())
                    for fh in range(NCH1 // 4):
                        begin(T)
                        for i in range(4):
                            c = fh * 4 + i
                            T.transpose(pb[i][0:w, 0:128],
                                        hb7_f[:, c * 322 + s0: c * 322 + s0 + w],
                                        ident[:, :])
                        step(T.drain())
                        begin(V)
                        for i in range(4):
                            evac_tp(XT[:, i * 128:(i + 1) * 128], i, w)
                        step(V.drain())
                        act_quant_tm(XT, 512, 54, 57)
                        begin(T)
                        for i in range(4):
                            T.transpose(pb[i][:, 0:128],
                                        XT[:, i * 128:(i + 1) * 128], ident[:, :])
                        step(T.drain())
                        begin(V)
                        for i in range(4):
                            c = fh * 4 + i
                            V.tensor_copy(hb7_f[:, c * 322 + s0: c * 322 + s0 + w],
                                          pb[i][0:128, 0:w])
                        step(V.drain())
                for m in range(NCH):
                    for kg in range(0, NCH1, 4):
                        begin(Sy)
                        dma(wnat[:, 0:512].rearrange("p (a b) -> p a b", b=128),
                            w2T[:, kg:kg + 4, m * 128:(m + 1) * 128])
                        sy_step()
                        begin(T)
                        for i in range(4):
                            T.matmul(pb[m][:, 0:tn],
                                     wnat[:, i * 128:(i + 1) * 128],
                                     hb7_f[:, (kg + i) * 322:(kg + i) * 322 + tn],
                                     start=(kg + i == 0),
                                     stop=(kg + i == NCH1 - 1))
                        step(T.drain())
                for m in range(NCH):
                    begin(V)
                    step(V.tensor_add(xs[:, m, t0:t0 + tn],
                                      xs[:, m, t0:t0 + tn], pb[m][:, 0:tn]))
                emit_ln(t0, tn)

        begin(Sy)
        dma(y, xs[:, :, 1:1 + TC])
        sy_step()
        begin(Sy)
        step(Sy.nop())
    return nc


def kernel(**inputs):
    cores = []
    x = np.asarray(inputs["L0_attn_x"])
    for c in range(8):
        b, half = c // 2, c % 2
        t0 = half * TC
        xs_ = np.zeros((D, TT), dtype=np.float32)
        xs_[:, 1:1 + TC] = x[b, t0:t0 + TC, :].T
        if half == 0:
            xs_[:, TT - 1] = x[b, t0 + TC, :]
        else:
            xs_[:, 0] = x[b, t0 - 1, :]
        xpre_ = np.ascontiguousarray(x[b, 0:TC, :].T)
        fM = 1.0 if half == 1 else 0.0
        fL = 1.0 if half == 1 else 0.0
        fR = 1.0 if half == 0 else 0.0
        im = {
            "xf": xs_.reshape(NCH, 128, TT).transpose(1, 0, 2).copy(),
            "xpre": xpre_.reshape(NCH, 128, TC).transpose(1, 0, 2).copy(),
            "flags": np.array([[fM, fL, fR, 0.0]], dtype=np.float32),
        }
        for nm in ["L0_attn_Wq", "L0_attn_Wk", "L0_attn_Wv", "L0_attn_Wo",
                   "L0_ffn_w_gate", "L0_ffn_w_up", "L0_ffn_w_down",
                   "L1_attn_Wq", "L1_attn_Wk", "L1_attn_Wv",
                   "L1_ffn_w1", "L1_ffn_w2"]:
            im[nm] = np.ascontiguousarray(np.asarray(inputs[nm], dtype=np.float32))
        im["L0_attn_betas"] = np.asarray(inputs["L0_attn_betas"], np.float32).reshape(H, DV)
        im["L0_attn_memory0"] = np.asarray(inputs["L0_attn_memory0"], np.float32).reshape(H, DK, DV)
        im["L0_attn_z0"] = np.asarray(inputs["L0_attn_z0"], np.float32).reshape(H, DK)
        im["L1_attn_bq"] = np.asarray(inputs["L1_attn_bq"], np.float32).reshape(1, D)
        im["L1_attn_bk"] = np.asarray(inputs["L1_attn_bk"], np.float32).reshape(1, D)
        im["L1_attn_bv"] = np.asarray(inputs["L1_attn_bv"], np.float32).reshape(1, D)
        im["L1_attn_weights"] = np.asarray(inputs["L1_attn_weights"], np.float32)
        im["L1_ffn_conv_w"] = np.asarray(inputs["L1_ffn_conv_w"], np.float32).reshape(DFF1, 3)
        im["L1_ffn_conv_b"] = np.asarray(inputs["L1_ffn_conv_b"], np.float32).reshape(1, DFF1)
        cores.append(im)
    nc = build_nc(taps=_DEBUG_TAPS)
    res = run_bass_kernel_spmd(nc, cores, core_ids=list(range(8)))
    kernel._last = res
    out = np.empty((B, S, D), dtype=np.float32)
    for c in range(8):
        b, half = c // 2, c % 2
        yc = np.asarray(res.results[c]["y"])
        out[b, half * TC:(half + 1) * TC, :] = \
            yc.transpose(1, 0, 2).reshape(D, TC).T
    return out
